# version 08
# baseline (speedup 1.0000x reference)
import numpy as np
import ml_dtypes
from contextlib import ExitStack

import concourse.bass as bass
import concourse.mybir as mybir
from concourse.bass_utils import run_bass_kernel_spmd

F32 = mybir.dt.float32
BF16 = mybir.dt.bfloat16
ALU = mybir.AluOpType
AF = mybir.ActivationFunctionType
AX = mybir.AxisListType
BF = ml_dtypes.bfloat16

M1 = 12582912.0
M2 = 6755399441055744.0
M6 = 6442450944.0

B, L, D = 4, 2048, 2048
DOWN, UPD, H, RD, VD = 512, 1024, 16, 32, 64
FI = 8192
TQ = 1024
SM_SCALE = float(1.0 / (16 ** 0.5 + 32 ** 0.5))
CSHIFT = 12.0


def build_nc():
    nc = bass.Bass("TRN2", target_bir_lowering=False, detect_race_conditions=False)

    d_x = nc.dram_tensor("x", [L, D], F32, kind="ExternalInput").ap()
    d_wdkv = nc.dram_tensor("wdkv", [D, DOWN], F32, kind="ExternalInput").ap()
    d_wuk = nc.dram_tensor("wuk", [DOWN, UPD], F32, kind="ExternalInput").ap()
    d_wuv = nc.dram_tensor("wuv", [DOWN, UPD], F32, kind="ExternalInput").ap()
    d_wdq = nc.dram_tensor("wdq", [D, DOWN], F32, kind="ExternalInput").ap()
    d_wuq = nc.dram_tensor("wuq", [DOWN, UPD], F32, kind="ExternalInput").ap()
    d_wqr = nc.dram_tensor("wqr", [DOWN, DOWN], F32, kind="ExternalInput").ap()
    d_wkr = nc.dram_tensor("wkr", [D, RD], F32, kind="ExternalInput").ap()
    d_wfc = nc.dram_tensor("wfc", [UPD, D], F32, kind="ExternalInput").ap()
    d_bdkv = nc.dram_tensor("bdkv_t", [128, 4], F32, kind="ExternalInput").ap()
    d_buk = nc.dram_tensor("buk_t", [128, 8], F32, kind="ExternalInput").ap()
    d_bdq = nc.dram_tensor("bdq_t", [128, 4], F32, kind="ExternalInput").ap()
    d_buq = nc.dram_tensor("buq_t", [128, 8], F32, kind="ExternalInput").ap()
    d_bqr = nc.dram_tensor("bqr_t", [128, 4], F32, kind="ExternalInput").ap()
    d_bkr = nc.dram_tensor("bkr_t", [32, 1], F32, kind="ExternalInput").ap()
    d_buv = nc.dram_tensor("buv_r", [1, UPD], F32, kind="ExternalInput").ap()
    d_bfc = nc.dram_tensor("bfc_r", [1, D], F32, kind="ExternalInput").ap()
    d_wgp = nc.dram_tensor("wg_p", [64 * D, 128], F32, kind="ExternalInput").ap()
    d_wup = nc.dram_tensor("wu_p", [64 * D, 128], F32, kind="ExternalInput").ap()
    d_wdp = nc.dram_tensor("wd_p", [16 * FI, 128], F32, kind="ExternalInput").ap()
    d_lnw = nc.dram_tensor("lnw_t", [128, 64], F32, kind="ExternalInput").ap()
    d_tqa = nc.dram_tensor("tqa", [DOWN, TQ], F32, kind="ExternalInput").ap()
    d_tqb = nc.dram_tensor("tqb", [DOWN, TQ], F32, kind="ExternalInput").ap()
    d_tka = nc.dram_tensor("tka", [RD, L], F32, kind="ExternalInput").ap()
    d_tkb = nc.dram_tensor("tkb", [RD, L], F32, kind="ExternalInput").ap()
    d_pqe_id = nc.dram_tensor("pqe_id", [128, 128], BF16, kind="ExternalInput").ap()
    d_pqe_rt = nc.dram_tensor("pqe_rt", [128, 128], BF16, kind="ExternalInput").ap()
    d_pqo_id = nc.dram_tensor("pqo_id", [128, 128], BF16, kind="ExternalInput").ap()
    d_pqo_rt = nc.dram_tensor("pqo_rt", [128, 128], BF16, kind="ExternalInput").ap()
    d_pk_rt = nc.dram_tensor("pk_rt", [32, 32], BF16, kind="ExternalInput").ap()
    d_identb = nc.dram_tensor("identb_in", [128, 128], BF16, kind="ExternalInput").ap()

    d_out = nc.dram_tensor("out", [TQ, D], F32, kind="ExternalOutput").ap()
    import os as _os
    _DBG = bool(_os.environ.get("KERNEL_DEBUG"))
    d_dbg_x2 = (nc.dram_tensor("dbg_x2", [TQ, D], F32, kind="ExternalOutput").ap()
                if _DBG else None)
    d_dbg_st = (nc.dram_tensor("dbg_st", [3, TQ], F32, kind="ExternalOutput").ap()
                if _DBG else None)
    d_dbg_p8 = (nc.dram_tensor("dbg_p8", [4, D], F32, kind="ExternalOutput").ap()
                if _DBG else None)
    d_dbg_h = (nc.dram_tensor("dbg_h", [FI, TQ], F32, kind="ExternalOutput").ap()
               if _DBG else None)
    d_dbg_rh = (nc.dram_tensor("dbg_rh", [FI, TQ], BF16, kind="ExternalOutput").ap()
                if _DBG else None)
    d_dbg_y = (nc.dram_tensor("dbg_y", [D, TQ], BF16, kind="ExternalOutput").ap()
               if _DBG else None)
    d_dbg_cbt = (nc.dram_tensor("dbg_cbt", [384, TQ], F32, kind="ExternalOutput").ap()
                 if _DBG else None)

    d_xn = nc.dram_tensor("xn_d", [L, D], BF16, kind="Internal").ap()
    d_rx = nc.dram_tensor("rx_d", [TQ, D], BF16, kind="Internal").ap()
    d_h = nc.dram_tensor("h_d", [FI, TQ], F32, kind="Internal").ap()
    d_y = nc.dram_tensor("y_d", [D, TQ], BF16, kind="Internal").ap()
    d_x2 = nc.dram_tensor("x2_d", [TQ, D], F32, kind="Internal").ap()
    d_zx = nc.dram_tensor("zx_d", [256, TQ], BF16, kind="Internal").ap()
    d_fl = nc.dram_tensor("fl_d", [48, 128], BF16, kind="Internal").ap()
    d_rh = nc.dram_tensor("rh_d", [FI, TQ], BF16, kind="Internal").ap()
    d_st = nc.dram_tensor("st_d", [3, TQ], F32, kind="Internal").ap()
    d_cbt = nc.dram_tensor("cbt_d", [384, TQ], F32, kind="Internal").ap()

    ctx = ExitStack()
    with ctx:
        E = ctx.enter_context
        _semn = [0]

        def sem():
            _semn[0] += 1
            return E(nc.semaphore(f"s{_semn[0]}"))

        onesK = E(nc.sbuf_tensor("onesK", [128, 1], BF16))
        ones1 = E(nc.sbuf_tensor("ones1", [1, 128], BF16))
        c_smsc = E(nc.sbuf_tensor("c_smsc", [128, 1], F32))
        c_mC = E(nc.sbuf_tensor("c_mC", [128, 1], F32))
        c_nhalf = E(nc.sbuf_tensor("c_nhalf", [128, 1], F32))
        c_half = E(nc.sbuf_tensor("c_half", [128, 1], F32))
        c_tiny = E(nc.sbuf_tensor("c_tiny", [128, 1], F32))
        c_zero = E(nc.sbuf_tensor("c_zero", [128, 1], F32))
        c_one = E(nc.sbuf_tensor("c_one", [128, 1], F32))
        c_neg1 = E(nc.sbuf_tensor("c_neg1", [128, 1], F32))
        c_ifi = E(nc.sbuf_tensor("c_ifi", [128, 1], F32))
        c_eps6 = E(nc.sbuf_tensor("c_eps6", [128, 1], F32))
        b_dkv = E(nc.sbuf_tensor("b_dkv", [128, 4], F32))
        b_uk = E(nc.sbuf_tensor("b_uk", [128, 8], F32))
        b_dq = E(nc.sbuf_tensor("b_dq", [128, 4], F32))
        b_uq = E(nc.sbuf_tensor("b_uq", [128, 8], F32))
        b_qr = E(nc.sbuf_tensor("b_qr", [128, 4], F32))
        b_kr = E(nc.sbuf_tensor("b_kr", [32, 1], F32))
        lnw = E(nc.sbuf_tensor("lnw", [128, 64], F32))
        lnw2 = E(nc.sbuf_tensor("lnw2", [128, 64], F32))

        row_sc = E(nc.sbuf_tensor("row_sc", [1, TQ], F32))
        row_sc2 = E(nc.sbuf_tensor("row_sc2", [1, TQ], F32))
        row_hi = E(nc.sbuf_tensor("row_hi", [1, TQ], BF16))
        row_lo = E(nc.sbuf_tensor("row_lo", [1, TQ], BF16))
        row_f = E(nc.sbuf_tensor("row_f", [1, TQ], F32))
        r16h = E(nc.sbuf_tensor("r16h", [1, 16], BF16))
        r16m = E(nc.sbuf_tensor("r16m", [1, 16], BF16))
        r16l = E(nc.sbuf_tensor("r16l", [1, 16], BF16))
        r16f = E(nc.sbuf_tensor("r16f", [1, 16], F32))
        r16g = E(nc.sbuf_tensor("r16g", [1, 16], F32))
        row_md = E(nc.sbuf_tensor("row_md", [1, TQ], BF16))

        bc16 = E(nc.sbuf_tensor("bc16", [128, 16], F32))
        m_sc = E(nc.sbuf_tensor("m_sc", [1, 16], F32))

        cb0 = E(nc.sbuf_tensor("cb0", [128, TQ], F32))
        cb1 = E(nc.sbuf_tensor("cb1", [128, TQ], F32))
        cdt = E(nc.sbuf_tensor("cdt", [128, TQ], F32))

        ln_m = E(nc.sbuf_tensor("ln_m", [128, 16], F32))
        ln_r = E(nc.sbuf_tensor("ln_r", [128, 16], F32))
        ln_t1 = E(nc.sbuf_tensor("ln_t1", [128, 16], F32))
        ln_t2 = E(nc.sbuf_tensor("ln_t2", [128, 16], F32))
        ln_t3 = E(nc.sbuf_tensor("ln_t3", [128, 16], F32))
        ln_v = E(nc.sbuf_tensor("ln_v", [128, 16], F32))
        sx_tm = E(nc.sbuf_tensor("sx_tm", [128, 8], F32))
        isx_tm = E(nc.sbuf_tensor("isx_tm", [128, 8], F32))
        zm_tm = E(nc.sbuf_tensor("zm_tm", [128, 8], F32))
        st8_hi = E(nc.sbuf_tensor("st8_hi", [128, 8], BF16))
        st8_md = E(nc.sbuf_tensor("st8_md", [128, 8], BF16))
        st8_lo = E(nc.sbuf_tensor("st8_lo", [128, 8], BF16))
        st8_g = E(nc.sbuf_tensor("st8_g", [128, 8], F32))
        st8_f = E(nc.sbuf_tensor("st8_f", [128, 8], F32))
        fl8a = E(nc.sbuf_tensor("fl8a", [8, 128], BF16))
        fl8b = E(nc.sbuf_tensor("fl8b", [8, 128], BF16))
        fl8c = E(nc.sbuf_tensor("fl8c", [8, 128], BF16))
        identb = E(nc.sbuf_tensor("identb", [128, 128], BF16))
        zm_acc = E(nc.sbuf_tensor("zm_acc", [128, TQ], F32))

        psG = E(nc.psum_tensor("psG", [128, 1024], F32))
        psU = E(nc.psum_tensor("psU", [128, 1024], F32))
        psR = E(nc.psum_tensor("psR", [128, 1024], F32))
        psX = E(nc.psum_tensor("psX", [128, 512], F32))
        psT = E(nc.psum_tensor("psT", [128, 1024], BF16))
        PROT = [psG[:, 0:512], psG[:, 512:1024], psU[:, 0:512], psU[:, 512:1024]]

        def wload(e, dst, src2d, p, o):
            return e.dma_start(out=dst.rearrange("p (c o) -> p c o", o=o),
                               in_=src2d.rearrange("(c p) o -> p c o", p=p))

        block = E(nc.Block())

        class Arena:

            def __init__(self, base, top):
                self.free_list = [(base, top)]
                self.live = {}

            def alloc(self, name, shape, dtype, align=64, high=False):
                import concourse.mybir as _mb
                nbytes = 1
                for d in shape[1:]:
                    nbytes *= d
                nbytes *= _mb.dt.size(dtype)
                blocks = (list(enumerate(self.free_list))[::-1] if high
                          else list(enumerate(self.free_list)))
                for i, (lo, hi) in blocks:
                    off = ((hi - nbytes) // align * align if high
                           else (lo + align - 1) // align * align)
                    if off + nbytes <= hi and off >= lo:
                        if off > lo:
                            self.free_list[i] = (lo, off)
                            self.free_list.insert(i + 1, (off + nbytes, hi))
                        else:
                            self.free_list[i] = (off + nbytes, hi)
                        if self.free_list[i][0] >= self.free_list[i][1]:
                            del self.free_list[i]
                        t = nc.alloc_sbuf_tensor_at(name, shape, dtype, offset=off)
                        self.live[name] = (off, off + nbytes)
                        return t
                raise AssertionError(
                    f"arena OOM for {name} {shape}: need {nbytes}, free={self.free_list}, live={self.live}")

            def free(self, *names):
                for name in names:
                    lo, hi = self.live.pop(name)
                    self.free_list.append((lo, hi))
                self.free_list.sort()
                merged = []
                for lo, hi in self.free_list:
                    if merged and merged[-1][1] >= lo:
                        merged[-1] = (merged[-1][0], max(merged[-1][1], hi))
                    else:
                        merged.append((lo, hi))
                self.free_list = [tuple(x) for x in merged]

        A = Arena(nc.sbuf_base, nc.sbuf_top)

        bar = sem()
        barv = [0]

        vlk = sem()
        _vlkc = [0]

        def vs(e, instr):
            _vlkc[0] += 1
            instr.then_inc(vlk, 1)
            e.wait_ge(vlk, _vlkc[0])

        def barrier():
            barv[0] += 5
            tgt = barv[0]

            @block.sync
            def _(e):
                e.nop().then_inc(bar, 1)
                e.wait_ge(bar, tgt)

            @block.tensor
            def _(e):
                e.drain()
                e.nop().then_inc(bar, 1)
                e.wait_ge(bar, tgt)

            @block.vector
            def _(e):
                e.engine_nop().then_inc(bar, 1)
                e.wait_ge(bar, tgt)

            @block.scalar
            def _(e):
                e.nop().then_inc(bar, 1)
                e.wait_ge(bar, tgt)

            @block.gpsimd
            def _(e):
                e.engine_nop().then_inc(bar, 1)
                e.wait_ge(bar, tgt)

        buv_r = A.alloc("buv_rs", [1, UPD], F32)
        buv_hi = A.alloc("buv_hi", [1, UPD], BF16)
        buv_lo = A.alloc("buv_lo", [1, UPD], BF16)
        q0 = sem()

        @block.sync
        def _(e):
            e.dma_start(out=identb[:], in_=d_identb[:]).then_inc(q0, 16)
            e.dma_start(out=b_dkv[:], in_=d_bdkv[:]).then_inc(q0, 16)
            e.dma_start(out=b_uk[:], in_=d_buk[:]).then_inc(q0, 16)
            e.dma_start(out=b_dq[:], in_=d_bdq[:]).then_inc(q0, 16)
            e.dma_start(out=b_uq[:], in_=d_buq[:]).then_inc(q0, 16)
            e.dma_start(out=b_qr[:], in_=d_bqr[:]).then_inc(q0, 16)
            e.dma_start(out=b_kr[:], in_=d_bkr[:]).then_inc(q0, 16)
            e.dma_start(out=buv_r[:], in_=d_buv[:]).then_inc(q0, 16)
            e.dma_start(out=lnw[:], in_=d_lnw[:]).then_inc(q0, 16)
            e.wait_ge(q0, 144)

        @block.vector
        def _(e):
            e.memset(onesK[:], 1.0)
            e.memset(ones1[:], 1.0)
            e.memset(c_smsc[:], SM_SCALE)
            e.memset(c_mC[:], -CSHIFT)
            e.memset(c_nhalf[:], -0.5)
            e.memset(c_half[:], 0.5)
            e.memset(c_tiny[:], 1e-30)
            e.memset(c_zero[:], 0.0)
            e.memset(c_one[:], 1.0)
            e.memset(c_neg1[:], -1.0)
            e.memset(c_ifi[:], 1.0 / FI)
            e.memset(c_eps6[:], 1e-6)
            e.memset(zm_acc[:], 0.0)
            e.memset(m_sc[:], 0.0)
            e.wait_ge(q0, 144)
            e.tensor_copy(buv_hi[:], buv_r[:])
            e.tensor_copy(row_f[0:1, 0:UPD], buv_hi[:])
            e.tensor_sub(buv_lo[:], buv_r[:], row_f[0:1, 0:UPD])
            e.tensor_mul(lnw2[:], lnw[:], lnw[:])

        barrier()

        def emit_ln(n, src_col, eps, s_done, dscr):
            sv = sem()

            @block.vector
            def _(e):
                for i in range(n):
                    xc = src_col(i)
                    e.reduce_sum(ln_t1[:, i:i + 1], xc, AX.X)
                    stt = e.scalar_tensor_tensor(dscr[:], xc, 1.0, xc, ALU.bypass,
                                                 ALU.mult, accum_out=ln_t2[:, i:i + 1])
                    if i == n - 1:
                        vs(e, stt)
                vs(e, e.tensor_scalar_mul(ln_m[:, 0:n], ln_t1[:, 0:n], 1.0 / D))
                vs(e, e.tensor_scalar_mul(ln_t3[:, 0:n], ln_t2[:, 0:n], 1.0 / D))
                vs(e, e.tensor_mul(ln_t1[:, 0:n], ln_m[:, 0:n], ln_m[:, 0:n]))
                vs(e, e.tensor_sub(ln_t3[:, 0:n], ln_t3[:, 0:n], ln_t1[:, 0:n]))
                vs(e, e.tensor_scalar_add(ln_v[:, 0:n], ln_t3[:, 0:n], eps))
                e.engine_nop().then_inc(sv, 1)

            @block.scalar
            def _(e):
                e.wait_ge(sv, 1)
                e.activation(ln_t1[:, 0:n], ln_v[:, 0:n], AF.Ln, bias=c_zero[:])
                e.activation(ln_t2[:, 0:n], ln_t1[:, 0:n], AF.Exp, bias=c_zero[:],
                             scale=c_nhalf[:]).then_inc(sv, 1)

            @block.vector
            def _(e):
                e.wait_ge(sv, 2)
                vs(e, e.tensor_mul(ln_t1[:, 0:n], ln_t2[:, 0:n], ln_t2[:, 0:n]))
                vs(e, e.tensor_mul(ln_t1[:, 0:n], ln_t1[:, 0:n], ln_v[:, 0:n]))
                vs(e, e.tensor_scalar(ln_t1[:, 0:n], ln_t1[:, 0:n], -0.5, 1.5,
                                      ALU.mult, ALU.add))
                vs(e, e.tensor_mul(ln_r[:, 0:n], ln_t2[:, 0:n], ln_t1[:, 0:n]))
                e.engine_nop().then_inc(s_done, 1)

        def emit_bcast(src_row, dest, negate=False):
            sv = sem()

            @block.vector
            def _(e):
                e.tensor_copy(row_hi[:], src_row)
                e.tensor_copy(row_f[:], row_hi[:])
                e.tensor_sub(row_sc2[:], src_row, row_f[:])
                e.tensor_copy(row_md[:], row_sc2[:])
                e.tensor_copy(row_f[:], row_md[:])
                e.tensor_sub(row_sc2[:], row_sc2[:], row_f[:])
                e.tensor_copy(row_lo[:], row_sc2[:]).then_inc(sv, 1)
                e.wait_ge(sv, 3)

            @block.tensor
            def _(e):
                e.wait_ge(sv, 1)
                for tt in range(2):
                    e.matmul(psG[:, tt * 512:(tt + 1) * 512], ones1[0:1, :],
                             row_hi[0:1, tt * 512:(tt + 1) * 512], start=True, stop=False)
                    e.matmul(psG[:, tt * 512:(tt + 1) * 512], ones1[0:1, :],
                             row_md[0:1, tt * 512:(tt + 1) * 512], start=False, stop=False)
                    e.matmul(psG[:, tt * 512:(tt + 1) * 512], ones1[0:1, :],
                             row_lo[0:1, tt * 512:(tt + 1) * 512],
                             start=False, stop=True).then_inc(sv, 1)

            @block.scalar
            def _(e):
                e.wait_ge(sv, 3)
                for tt in range(2):
                    e.activation(dest[:, tt * 512:(tt + 1) * 512],
                                 psG[:, tt * 512:(tt + 1) * 512], AF.Identity,
                                 bias=c_zero[:],
                                 scale=(c_neg1[:] if negate else c_one[:])).then_inc(sv, 1)

            @block.tensor
            def _(e):
                e.wait_ge(sv, 5)

        def emit_flatten(st_tile, dest_row, fl_off):
            sv = sem()
            sq = sem()

            @block.vector
            def _(e):
                vs(e, e.tensor_copy(st8_hi[:], st_tile))
                vs(e, e.tensor_copy(st8_f[:], st8_hi[:]))
                vs(e, e.tensor_sub(st8_g[:], st_tile, st8_f[:]))
                vs(e, e.tensor_copy(st8_md[:], st8_g[:]))
                vs(e, e.tensor_copy(st8_f[:], st8_md[:]))
                vs(e, e.tensor_sub(st8_g[:], st8_g[:], st8_f[:]))
                vs(e, e.tensor_copy(st8_lo[:], st8_g[:]))
                e.engine_nop().then_inc(sv, 1)

            for ti, (src, dst) in enumerate(((st8_hi, fl8a), (st8_md, fl8c), (st8_lo, fl8b))):
                @block.tensor
                def _(e, src=src):
                    e.wait_ge(sv, 1 + 2 * ti)
                    e.transpose(psT[0:8, 0:128], src[:], identb[:]).then_inc(sv, 1)

                @block.scalar
                def _(e, dst=dst):
                    e.wait_ge(sv, 2 + 2 * ti)
                    e.copy(dst[:], psT[0:8, 0:128]).then_inc(sv, 1)

            @block.gpsimd
            def _(e):
                e.wait_ge(sv, 7)
                e.dma_start(out=d_fl[fl_off:fl_off + 8, :], in_=fl8a[:]).then_inc(sq, 16)
                e.dma_start(out=d_fl[fl_off + 8:fl_off + 16, :], in_=fl8c[:]).then_inc(sq, 16)
                e.dma_start(out=d_fl[fl_off + 16:fl_off + 24, :], in_=fl8b[:]).then_inc(sq, 16)
                e.wait_ge(sq, 48)
                e.dma_start(out=row_hi[:],
                            in_=d_fl[fl_off:fl_off + 8, :].rearrange("p n -> (p n)")[None, :]).then_inc(sq, 16)
                e.dma_start(out=row_md[:],
                            in_=d_fl[fl_off + 8:fl_off + 16, :].rearrange("p n -> (p n)")[None, :]).then_inc(sq, 16)
                e.dma_start(out=row_lo[:],
                            in_=d_fl[fl_off + 16:fl_off + 24, :].rearrange("p n -> (p n)")[None, :]).then_inc(sq, 16)
                e.engine_nop().then_inc(sv, 1)

            @block.vector
            def _(e):
                e.wait_ge(sv, 8)
                e.wait_ge(sq, 96)
                e.tensor_copy(row_f[:], row_hi[:])
                e.tensor_copy(row_sc2[:], row_md[:])
                e.tensor_add(row_f[:], row_f[:], row_sc2[:])
                e.tensor_copy(row_sc2[:], row_lo[:])
                e.tensor_add(dest_row, row_f[:], row_sc2[:])
                e.engine_nop().then_inc(sv, 1)

        xstr = A.alloc("xstr", [128, 2 * D], F32)
        dscr1 = A.alloc("dscr1", [128, D], F32)
        xnst = A.alloc("xnst", [128, 2 * D], BF16)
        q1 = sem()
        v1 = sem()
        q1b = sem()
        v1b = sem()
        g1 = sem()
        lnd1 = sem()

        @block.sync
        def _(e):
            for i in range(16):
                if i >= 2:
                    e.wait_ge(v1, i - 1)
                e.dma_start(out=xstr[:, (i % 2) * D:(i % 2 + 1) * D],
                            in_=d_x[i * 128:(i + 1) * 128, :]).then_inc(q1, 16)

        @block.vector
        def _(e):
            for i in range(16):
                e.wait_ge(q1, 16 * (i + 1))
                xc = xstr[:, (i % 2) * D:(i % 2 + 1) * D]
                e.reduce_sum(ln_t1[:, i:i + 1], xc, AX.X)
                stt = e.scalar_tensor_tensor(dscr1[:], xc, 1.0, xc, ALU.bypass,
                                             ALU.mult, accum_out=ln_t2[:, i:i + 1])
                if i == 15:
                    vs(e, stt)
                e.engine_nop().then_inc(v1, 1)
            vs(e, e.tensor_scalar_mul(ln_m[:], ln_t1[:], 1.0 / D))
            vs(e, e.tensor_scalar_mul(ln_t3[:], ln_t2[:], 1.0 / D))
            vs(e, e.tensor_mul(ln_t1[:], ln_m[:], ln_m[:]))
            vs(e, e.tensor_sub(ln_t3[:], ln_t3[:], ln_t1[:]))
            vs(e, e.tensor_scalar_add(ln_v[:], ln_t3[:], 1e-5))
            e.engine_nop().then_inc(lnd1, 1)

        @block.scalar
        def _(e):
            e.wait_ge(lnd1, 1)
            e.activation(ln_t1[:], ln_v[:], AF.Ln, bias=c_zero[:])
            e.activation(ln_t2[:], ln_t1[:], AF.Exp, bias=c_zero[:],
                         scale=c_nhalf[:]).then_inc(lnd1, 1)

        @block.vector
        def _(e):
            e.wait_ge(lnd1, 2)
            vs(e, e.tensor_mul(ln_t1[:], ln_t2[:], ln_t2[:]))
            vs(e, e.tensor_mul(ln_t1[:], ln_t1[:], ln_v[:]))
            vs(e, e.tensor_scalar(ln_t1[:], ln_t1[:], -0.5, 1.5, ALU.mult, ALU.add))
            vs(e, e.tensor_mul(ln_r[:], ln_t2[:], ln_t1[:]))
            e.engine_nop().then_inc(lnd1, 1)

        @block.sync
        def _(e):
            e.wait_ge(lnd1, 3)
            for i in range(16):
                if i >= 2:
                    e.wait_ge(v1b, i - 1)
                e.dma_start(out=xstr[:, (i % 2) * D:(i % 2 + 1) * D],
                            in_=d_x[i * 128:(i + 1) * 128, :]).then_inc(q1b, 16)

        @block.vector
        def _(e):
            for i in range(16):
                e.wait_ge(q1b, 16 * (i + 1))
                if i >= 2:
                    e.wait_ge(g1, 16 * (i - 1))
                e.tensor_scalar(xnst[:, (i % 2) * D:(i % 2 + 1) * D],
                                xstr[:, (i % 2) * D:(i % 2 + 1) * D],
                                ln_m[:, i:i + 1], ln_r[:, i:i + 1],
                                ALU.subtract, ALU.mult).then_inc(v1b, 1)

        @block.gpsimd
        def _(e):
            for i in range(16):
                e.wait_ge(v1b, i + 1)
                e.dma_start(out=d_xn[i * 128:(i + 1) * 128, :],
                            in_=xnst[:, (i % 2) * D:(i % 2 + 1) * D]).then_inc(g1, 16)
            e.wait_ge(g1, 256)

        barrier()
        A.free("xstr", "dscr1", "xnst")

        ckv = A.alloc("ckv", [128, 4 * L], BF16)
        cq = A.alloc("cq", [128, 4 * TQ], BF16)
        krp = A.alloc("krp", [32, L], F32)

        xn_fm = A.alloc("xn_fm", [128, 16 * L], BF16)
        wA = A.alloc("wA", [128, 16 * DOWN], BF16)
        wB = A.alloc("wB", [128, 16 * DOWN], BF16)
        wC = A.alloc("wC", [128, 16 * RD], BF16)
        q2 = sem()
        g2 = sem()
        t2 = sem()
        v2 = sem()

        @block.sync
        def _(e):
            for dch in range(16):
                e.dma_start_transpose(xn_fm[:, dch * L:(dch + 1) * L],
                                      d_xn[:, dch * 128:(dch + 1) * 128]).then_inc(q2, 16)

        @block.gpsimd
        def _(e):
            wload(e, wA[:], d_wdkv, 128, DOWN).then_inc(g2, 16)
            wload(e, wB[:], d_wdq, 128, DOWN).then_inc(g2, 16)
            wload(e, wC[:], d_wkr, 128, RD).then_inc(g2, 16)

        jobs2 = ([("ckv", mi, kt) for mi in range(4) for kt in range(4)]
                 + [("cq", mi, kt) for mi in range(4) for kt in range(2)]
                 + [("krp", 0, kt) for kt in range(4)])

        @block.tensor
        def _(e):
            e.wait_ge(q2, 256)
            e.wait_ge(g2, 48)
            for j, (kind, mi, kt) in enumerate(jobs2):
                pb = PROT[j % 4]
                if j >= 4:
                    e.wait_ge(v2, j - 3)
                for dch in range(16):
                    st, sp = dch == 0, dch == 15
                    if kind == "ckv":
                        m = e.matmul(pb, wA[:, dch * 512 + mi * 128:dch * 512 + (mi + 1) * 128],
                                     xn_fm[:, dch * L + kt * 512:dch * L + (kt + 1) * 512],
                                     start=st, stop=sp)
                    elif kind == "cq":
                        m = e.matmul(pb, wB[:, dch * 512 + mi * 128:dch * 512 + (mi + 1) * 128],
                                     xn_fm[:, dch * L + kt * 512:dch * L + (kt + 1) * 512],
                                     start=st, stop=sp)
                    else:
                        m = e.matmul(pb[0:RD, :], wC[:, dch * RD:(dch + 1) * RD],
                                     xn_fm[:, dch * L + kt * 512:dch * L + (kt + 1) * 512],
                                     start=st, stop=sp)
                m.then_inc(t2, 1)

        @block.scalar
        def _(e):
            for j, (kind, mi, kt) in enumerate(jobs2):
                pb = PROT[j % 4]
                e.wait_ge(t2, j + 1)
                if kind == "ckv":
                    e.activation(ckv[:, mi * L + kt * 512:mi * L + (kt + 1) * 512],
                                 pb, AF.Identity, bias=b_dkv[:, mi:mi + 1],
                                 scale=c_one[:]).then_inc(v2, 1)
                elif kind == "cq":
                    e.activation(cq[:, mi * TQ + kt * 512:mi * TQ + (kt + 1) * 512],
                                 pb, AF.Identity, bias=b_dq[:, mi:mi + 1],
                                 scale=c_one[:]).then_inc(v2, 1)
                else:
                    e.activation(krp[:, kt * 512:(kt + 1) * 512],
                                 pb[0:RD, :], AF.Identity, bias=b_kr[:, 0:1],
                                 scale=c_one[0:RD, :]).then_inc(v2, 1)

        barrier()
        A.free("xn_fm", "wA", "wB", "wC")

        kc = A.alloc("kc", [128, 8 * L], BF16)
        vsb = A.alloc("vsb", [128, 16 * 1040], BF16)
        krope = A.alloc("krope", [128, L], BF16)

        wUK = A.alloc("wUK", [128, 4 * UPD], BF16)
        wUV = A.alloc("wUV", [128, 4 * UPD], BF16)
        tka = A.alloc("tka_s", [32, L], F32)
        tkb = A.alloc("tkb_s", [32, L], F32)
        pk_rt = A.alloc("pk_rt_s", [32, 32], BF16)
        rk1 = A.alloc("rk1", [32, L], F32)
        rk2 = A.alloc("rk2", [32, L], BF16)
        kscx = A.alloc("kscx", [32, 1024], F32)
        gb1 = sem()
        qb1 = sem()
        tb1 = sem()
        vrk = sem()
        sv1 = sem()
        sk1 = sem()
        vk1 = sem()

        @block.gpsimd
        def _(e):
            wload(e, wUK[:], d_wuk, 128, UPD).then_inc(gb1, 16)
            wload(e, wUV[:], d_wuv, 128, UPD).then_inc(gb1, 16)

        @block.sync
        def _(e):
            e.dma_start(out=tka[:], in_=d_tka[:]).then_inc(qb1, 16)
            e.dma_start(out=tkb[:], in_=d_tkb[:]).then_inc(qb1, 16)
            e.dma_start(out=pk_rt[:], in_=d_pk_rt[:]).then_inc(qb1, 16)

        jobsB1 = ([("kc", mi, kt) for mi in range(8) for kt in range(4)]
                  + [("v", kch, vt) for kch in range(16) for vt in range(2)])

        @block.vector
        def _(e):
            e.memset(vsb[:], 1.0)
            e.wait_ge(qb1, 48)
            e.tensor_mul(rk1[:], krp[:], tka[:])
            e.tensor_mul(rk2[:], krp[:], tkb[:]).then_inc(vrk, 1)
            for kt in range(4):
                e.wait_ge(sk1, kt + 1)
                e.tensor_add(krope[0:RD, kt * 512:(kt + 1) * 512],
                             rk1[:, kt * 512:(kt + 1) * 512],
                             kscx[:, (kt % 2) * 512:(kt % 2 + 1) * 512])
                e.tensor_copy(krope[64:64 + RD, kt * 512:(kt + 1) * 512],
                              krope[0:RD, kt * 512:(kt + 1) * 512])
                e.engine_nop().then_inc(vk1, 1)

        @block.scalar
        def _(e):
            e.wait_ge(vrk, 1)
            for j, (kind, a, b2) in enumerate(jobsB1):
                pb = PROT[j % 4]
                e.wait_ge(tb1, j + 1)
                if kind == "kc":
                    e.activation(kc[:, a * L + b2 * 512:a * L + (b2 + 1) * 512],
                                 pb, AF.Identity, bias=b_uk[:, a:a + 1],
                                 scale=c_one[:]).then_inc(sv1, 1)
                else:
                    dst = vsb[:, a * 1040 + b2 * 520:a * 1040 + (b2 + 1) * 520]
                    dv = dst.rearrange("p (h q) -> p h q", q=65)[:, :, 0:64]
                    e.copy(dv, pb).then_inc(sv1, 1)
            for kt in range(4):
                e.wait_ge(tb1, 64 + kt + 1)
                if kt >= 2:
                    e.wait_ge(vk1, kt - 1)
                e.copy(kscx[:, (kt % 2) * 512:(kt % 2 + 1) * 512],
                       psX[0:RD, :]).then_inc(sk1, 1)

        @block.tensor
        def _(e):
            e.wait_ge(gb1, 32)
            for j, (kind, a, b2) in enumerate(jobsB1):
                pb = PROT[j % 4]
                if j >= 4:
                    e.wait_ge(sv1, j - 3)
                if kind == "kc":
                    for ci in range(4):
                        m = e.matmul(pb, wUK[:, ci * UPD + a * 128:ci * UPD + (a + 1) * 128],
                                     ckv[:, ci * L + b2 * 512:ci * L + (b2 + 1) * 512],
                                     start=(ci == 0), stop=(ci == 3))
                else:
                    for ci in range(4):
                        e.matmul(pb, ckv[:, ci * L + a * 128:ci * L + (a + 1) * 128],
                                 wUV[:, ci * UPD + b2 * 512:ci * UPD + (b2 + 1) * 512],
                                 start=(ci == 0), stop=False)
                    e.matmul(pb, ones1[0:1, :], buv_hi[0:1, b2 * 512:(b2 + 1) * 512],
                             start=False, stop=False)
                    m = e.matmul(pb, ones1[0:1, :], buv_lo[0:1, b2 * 512:(b2 + 1) * 512],
                                 start=False, stop=True)
                m.then_inc(tb1, 1)
            e.wait_ge(vrk, 1)
            for kt in range(4):
                if kt >= 1:
                    e.wait_ge(sk1, kt)
                e.matmul(psX[0:RD, :], pk_rt[:], rk2[:, kt * 512:(kt + 1) * 512],
                         start=True, stop=True).then_inc(tb1, 1)

        barrier()
        A.free("wUK", "wUV", "tka_s", "tkb_s", "pk_rt_s", "rk1", "rk2", "kscx")
        A.free("buv_rs", "buv_hi", "buv_lo")
        A.free("ckv", "krp")

        qc = A.alloc("qc", [128, 8 * TQ], BF16)
        qrope = A.alloc("qrope", [128, 8 * TQ], BF16)
        wUQ = A.alloc("wUQ", [128, 4 * UPD], BF16)
        wQR = A.alloc("wQR", [128, 4 * DOWN], BF16)
        pqe_id = A.alloc("pqe_id_s", [128, 128], BF16)
        pqe_rt = A.alloc("pqe_rt_s", [128, 128], BF16)
        pqo_id = A.alloc("pqo_id_s", [128, 128], BF16)
        pqo_rt = A.alloc("pqo_rt_s", [128, 128], BF16)
        qrpc = A.alloc("qrpc", [128, TQ], F32)
        tqac = A.alloc("tqac", [128, 2 * TQ], F32)
        tqbc = A.alloc("tqbc", [128, 2 * TQ], F32)
        rt1 = A.alloc("rt1", [128, TQ], BF16)
        rt2 = A.alloc("rt2", [128, TQ], BF16)
        gb2 = sem()
        qb2 = sem()
        tb2 = sem()
        vb2 = sem()
        tqr = sem()
        vqr = sem()
        vrt = sem()
        trp = sem()
        vqe = sem()

        @block.gpsimd
        def _(e):
            wload(e, wUQ[:], d_wuq, 128, UPD).then_inc(gb2, 16)
            wload(e, wQR[:], d_wqr, 128, DOWN).then_inc(gb2, 16)
            e.dma_start(out=pqe_id[:], in_=d_pqe_id[:]).then_inc(gb2, 16)
            e.dma_start(out=pqe_rt[:], in_=d_pqe_rt[:]).then_inc(gb2, 16)
            e.dma_start(out=pqo_id[:], in_=d_pqo_id[:]).then_inc(gb2, 16)
            e.dma_start(out=pqo_rt[:], in_=d_pqo_rt[:]).then_inc(gb2, 16)

        @block.sync
        def _(e):
            for mi in range(4):
                if mi >= 2:
                    e.wait_ge(vrt, mi - 1)
                e.dma_start(out=tqac[:, (mi % 2) * TQ:(mi % 2 + 1) * TQ],
                            in_=d_tqa[mi * 128:(mi + 1) * 128, :]).then_inc(qb2, 16)
                e.dma_start(out=tqbc[:, (mi % 2) * TQ:(mi % 2 + 1) * TQ],
                            in_=d_tqb[mi * 128:(mi + 1) * 128, :]).then_inc(qb2, 16)

        @block.tensor
        def _(e):
            e.wait_ge(gb2, 96)
            for j in range(16):
                mi, kt = j // 2, j % 2
                pb = PROT[j % 4]
                if j >= 4:
                    e.wait_ge(vb2, j - 3)
                for ci in range(4):
                    m = e.matmul(pb, wUQ[:, ci * UPD + mi * 128:ci * UPD + (mi + 1) * 128],
                                 cq[:, ci * TQ + kt * 512:ci * TQ + (kt + 1) * 512],
                                 start=(ci == 0), stop=(ci == 3))
                m.then_inc(tb2, 1)
            e.wait_ge(vb2, 16)
            for mi in range(4):
                e.wait_ge(vqr, 2 * mi)
                e.wait_ge(vqe, 4 * mi)
                for kt in range(2):
                    for ci in range(4):
                        m = e.matmul(psU[:, kt * 512:(kt + 1) * 512],
                                     wQR[:, ci * DOWN + mi * 128:ci * DOWN + (mi + 1) * 128],
                                     cq[:, ci * TQ + kt * 512:ci * TQ + (kt + 1) * 512],
                                     start=(ci == 0), stop=(ci == 3))
                    m.then_inc(tqr, 1)
                e.wait_ge(vrt, mi + 1)
                for half, (pid, prt) in enumerate(((pqe_id, pqe_rt), (pqo_id, pqo_rt))):
                    tgt = psG if half == 0 else psU
                    for tt in range(2):
                        e.matmul(tgt[:, tt * 512:(tt + 1) * 512], pid[:],
                                 rt1[:, tt * 512:(tt + 1) * 512], start=True, stop=False)
                        e.matmul(tgt[:, tt * 512:(tt + 1) * 512], prt[:],
                                 rt2[:, tt * 512:(tt + 1) * 512],
                                 start=False, stop=True).then_inc(trp, 1)

        @block.scalar
        def _(e):
            for j in range(16):
                mi, kt = j // 2, j % 2
                pb = PROT[j % 4]
                e.wait_ge(tb2, j + 1)
                e.activation(qc[:, mi * TQ + kt * 512:mi * TQ + (kt + 1) * 512],
                             pb, AF.Identity, bias=b_uq[:, mi:mi + 1],
                             scale=c_one[:]).then_inc(vb2, 1)
            for mi in range(4):
                e.wait_ge(tqr, 2 * (mi + 1))
                if mi >= 1:
                    e.wait_ge(vrt, mi)
                for kt in range(2):
                    e.activation(qrpc[:, kt * 512:(kt + 1) * 512],
                                 psU[:, kt * 512:(kt + 1) * 512], AF.Identity,
                                 bias=b_qr[:, mi:mi + 1],
                                 scale=c_one[:]).then_inc(vqr, 1)
                e.wait_ge(trp, 4 * (mi + 1))
                for half in range(2):
                    tgt = psG if half == 0 else psU
                    oc = 2 * mi + half
                    e.copy(qrope[:, oc * TQ:oc * TQ + 512],
                           tgt[:, 0:512]).then_inc(vqe, 1)
                    e.copy(qrope[:, oc * TQ + 512:(oc + 1) * TQ],
                           tgt[:, 512:1024]).then_inc(vqe, 1)

        @block.vector
        def _(e):
            for mi in range(4):
                e.wait_ge(vqr, 2 * (mi + 1))
                e.wait_ge(qb2, 32 * (mi + 1))
                if mi >= 1:
                    e.wait_ge(trp, 4 * mi)
                sl = slice((mi % 2) * TQ, (mi % 2 + 1) * TQ)
                e.tensor_mul(rt1[:], qrpc[:], tqac[:, sl])
                e.tensor_mul(rt2[:], qrpc[:], tqbc[:, sl]).then_inc(vrt, 1)

        barrier()
        A.free("wUQ", "wQR", "pqe_id_s", "pqe_rt_s", "pqo_id_s", "pqo_rt_s", "qrpc", "tqac", "tqbc", "rt1", "rt2")
        A.free("cq")

        attnO = A.alloc("attnO", [128, 8 * TQ], BF16, high=True)
        attnOl = A.alloc("attnOl", [128, 8 * TQ], BF16, high=True)
        of32 = A.alloc("of32", [64, 512], F32)
        of32b = A.alloc("of32b", [64, 512], F32)

        asb = A.alloc("asb", [128, 2 * 512], BF16)
        bcsb = A.alloc("bcsb", [64, 512], F32)
        osb = A.alloc("osb", [64, 512], F32)
        r512a = A.alloc("r512a", [1, 512], F32)
        r512s = A.alloc("r512s", [1, 512], F32)
        r512h = A.alloc("r512h", [1, 512], BF16)
        r512l = A.alloc("r512l", [1, 512], BF16)
        r512f = A.alloc("r512f", [1, 512], F32)
        tS = sem()
        sE = sem()
        tA = sem()
        tO = sem()
        sO = sem()
        vR = sem()
        tB = sem()
        sP = sem()
        vO = sem()

        its = [(h, qt) for h in range(H) for qt in range(2)]

        @block.tensor
        def _(e):
            for it, (h, qt) in enumerate(its):
                ch = h // 2
                rb = (h % 2) * 64
                qs = qt * 512
                vcol = h * 65
                for kci in range(17):
                    if kci < 16:
                        gid = it * 16 + kci
                        if gid >= 2:
                            e.wait_ge(sE, gid - 1)
                        pb = psG[:, (kci % 2) * 512:(kci % 2 + 1) * 512]
                        e.matmul(pb,
                                 kc[rb:rb + 64, ch * L + kci * 128:ch * L + (kci + 1) * 128],
                                 qc[rb:rb + 64, ch * TQ + qs:ch * TQ + qs + 512],
                                 start=True, stop=False)
                        e.matmul(pb, krope[rb:rb + RD, kci * 128:(kci + 1) * 128],
                                 qrope[rb:rb + RD, ch * TQ + qs:ch * TQ + qs + 512],
                                 start=False, stop=True).then_inc(tS, 1)
                    if kci >= 1:
                        kco = kci - 1
                        e.wait_ge(sE, it * 16 + kco + 1)
                        if kco == 0:
                            e.wait_ge(sP, 2 * it)
                        e.matmul(psU[0:65, 0:512],
                                 vsb[:, kco * 1040 + vcol:kco * 1040 + vcol + 65],
                                 asb[:, (kco % 2) * 512:(kco % 2 + 1) * 512],
                                 start=(kco == 0), stop=(kco == 15)).then_inc(tA, 1)
                e.drain()
                e.nop().then_inc(tO, 1)
                e.wait_ge(vR, it + 1)
                e.matmul(psU[0:64, 512:1024], ones1[0:1, 0:64], r512h[0:1, :],
                         start=True, stop=False)
                e.matmul(psU[0:64, 512:1024], ones1[0:1, 0:64], r512l[0:1, :],
                         start=False, stop=True).then_inc(tB, 1)

        @block.scalar
        def _(e):
            for it in range(len(its)):
                for kci in range(16):
                    gid = it * 16 + kci
                    e.wait_ge(tS, gid + 1)
                    e.wait_ge(tA, it * 16 + max(0, kci - 1))
                    pb = psG[:, (kci % 2) * 512:(kci % 2 + 1) * 512]
                    e.activation(asb[:, (kci % 2) * 512:(kci % 2 + 1) * 512], pb,
                                 AF.Exp, bias=c_mC[:], scale=c_smsc[:]).then_inc(sE, 1)
                e.wait_ge(tO, it + 1)
                e.copy(r512s[:], psU[64:65, 0:512]).then_inc(sO, 1)
                e.wait_ge(tB, it + 1)
                if it >= 1:
                    e.wait_ge(vO, it)
                e.copy(bcsb[:], psU[0:64, 512:1024]).then_inc(sP, 1)
                e.copy(osb[:], psU[0:64, 0:512]).then_inc(sP, 1)

        @block.vector
        def _(e):
            for it, (h, qt) in enumerate(its):
                ch = h // 2
                rb = (h % 2) * 64
                qs = qt * 512
                e.wait_ge(sO, it + 1)
                vs(e, e.reciprocal(r512a[:], r512s[:]))
                vs(e, e.tensor_copy(r512h[:], r512a[:]))
                vs(e, e.tensor_copy(r512f[:], r512h[:]))
                e.tensor_sub(r512l[:], r512a[:], r512f[:]).then_inc(vR, 1)
                e.wait_ge(sP, 2 * (it + 1))
                sl5 = slice(ch * TQ + qs, ch * TQ + qs + 512)
                e.tensor_mul(of32[:], osb[:], bcsb[:])
                e.tensor_copy(attnO[rb:rb + 64, sl5], of32[:])
                e.tensor_copy(of32b[:], attnO[rb:rb + 64, sl5])
                e.tensor_sub(of32b[:], of32[:], of32b[:])
                e.tensor_copy(attnOl[rb:rb + 64, sl5], of32b[:]).then_inc(vO, 1)

        barrier()
        A.free("asb", "bcsb", "osb", "of32", "of32b", "r512a", "r512s", "r512h", "r512l", "r512f")
        A.free("kc", "vsb", "krope", "qc", "qrope")
        wFC = A.alloc("wFC", [128, 8 * D], BF16)
        g5 = sem()

        @block.gpsimd
        def _(e):
            wload(e, wFC[:], d_wfc, 128, D).then_inc(g5, 16)

        x2sb = A.alloc("x2sb", [128, 8 * D], F32)
        xqst = A.alloc("xqst", [128, 2 * D], F32)
        p6s = A.alloc("p6s", [128, 1024], F32)
        bfc_r = A.alloc("bfc_rs", [1, D], F32)
        bfc_hi = A.alloc("bfc_hi", [1, D], BF16)
        bfc_lo = A.alloc("bfc_lo", [1, D], BF16)
        q6 = sem()
        q6b = sem()
        t6 = sem()
        s6 = sem()
        v6 = sem()

        @block.sync
        def _(e):
            e.dma_start(out=bfc_r[:], in_=d_bfc[:]).then_inc(q6b, 16)
            for tc in range(8):
                if tc >= 2:
                    e.wait_ge(v6, 1 + 4 * (tc - 1))
                e.dma_start(out=xqst[:, (tc % 2) * D:(tc % 2 + 1) * D],
                            in_=d_x[tc * 128:(tc + 1) * 128, :]).then_inc(q6, 16)

        @block.scalar
        def _(e):
            for tc in range(8):
                for ot in range(4):
                    idx = 4 * tc + ot
                    e.wait_ge(t6, idx + 1)
                    if idx >= 2:
                        e.wait_ge(v6, idx)
                    e.copy(p6s[:, (idx % 2) * 512:(idx % 2 + 1) * 512],
                           PROT[ot]).then_inc(s6, 1)

        @block.vector
        def _(e):
            e.wait_ge(q6b, 16)
            for bh in range(2):
                slb = slice(bh * 1024, (bh + 1) * 1024)
                e.tensor_copy(bfc_hi[0:1, slb], bfc_r[0:1, slb])
                e.tensor_copy(row_f[:], bfc_hi[0:1, slb])
                m6 = e.tensor_sub(bfc_lo[0:1, slb], bfc_r[0:1, slb], row_f[:])
            m6.then_inc(v6, 1)
            for tc in range(8):
                for ot in range(4):
                    idx = 4 * tc + ot
                    e.wait_ge(s6, idx + 1)
                    e.tensor_add(x2sb[:, tc * D + ot * 512:tc * D + (ot + 1) * 512],
                                 p6s[:, (idx % 2) * 512:(idx % 2 + 1) * 512],
                                 xqst[:, (tc % 2) * D + ot * 512:(tc % 2) * D + (ot + 1) * 512]
                                 ).then_inc(v6, 1)

        @block.tensor
        def _(e):
            e.wait_ge(g5, 16)
            e.wait_ge(v6, 1)
            for tc in range(8):
                e.wait_ge(q6, 16 * (tc + 1))
                for ot in range(4):
                    if tc >= 1:
                        e.wait_ge(s6, 4 * (tc - 1) + ot + 1)
                    pb = PROT[ot]
                    for f in range(8):
                        e.matmul(pb, attnO[:, f * TQ + tc * 128:f * TQ + (tc + 1) * 128],
                                 wFC[:, f * D + ot * 512:f * D + (ot + 1) * 512],
                                 start=(f == 0), stop=False)
                    for f in range(8):
                        e.matmul(pb, attnOl[:, f * TQ + tc * 128:f * TQ + (tc + 1) * 128],
                                 wFC[:, f * D + ot * 512:f * D + (ot + 1) * 512],
                                 start=False, stop=False)
                    e.matmul(pb, ones1[0:1, :], bfc_hi[0:1, ot * 512:(ot + 1) * 512],
                             start=False, stop=False)
                    e.matmul(pb, ones1[0:1, :], bfc_lo[0:1, ot * 512:(ot + 1) * 512],
                             start=False, stop=True).then_inc(t6, 1)

        barrier()
        A.free("xqst", "p6s", "bfc_rs", "bfc_hi", "bfc_lo")
        A.free("attnO", "attnOl", "wFC")

        rstd_row = A.alloc("rstd_row", [1, TQ], F32)
        invsh_row = A.alloc("invsh_row", [1, TQ], F32)
        isx_row = A.alloc("isx_row", [1, TQ], F32)
        cb1_row = A.alloc("cb1_row", [1, TQ], F32)
        cd_row = A.alloc("cd_row", [1, TQ], F32)
        zx_row = A.alloc("zx_row", [1, TQ], F32)

        dscr7 = A.alloc("dscr7", [128, D], F32)
        x2nc = A.alloc("x2nc", [128, D], F32)
        sA7 = A.alloc("sA7", [128, D], F32)
        sB7 = A.alloc("sB7", [128, D], F32)
        sC7 = A.alloc("sC7", [128, D], F32)
        rxst = A.alloc("rxst", [128, 2 * D], BF16)
        lnd2 = sem()
        v7 = sem()
        g7 = sem()

        emit_ln(8, lambda i: x2sb[:, i * D:(i + 1) * D], 1e-5, lnd2, dscr7)

        @block.vector
        def _(e):
            e.wait_ge(lnd2, 1)
            for j in range(8):
                if j >= 2:
                    e.wait_ge(g7, 16 * (j - 1))
                e.tensor_scalar(x2nc[:], x2sb[:, j * D:(j + 1) * D],
                                ln_m[:, j:j + 1], ln_r[:, j:j + 1], ALU.subtract, ALU.mult)
                vs(e, e.reduce_max(st8_f[:, j:j + 1], x2nc[:], AX.X,
                                   apply_absolute_value=True))
                vs(e, e.tensor_scalar_max(st8_f[:, j:j + 1], st8_f[:, j:j + 1], 1e-5))
                vs(e, e.tensor_scalar_mul(st8_f[:, j:j + 1], st8_f[:, j:j + 1], 1.0 / 127.0))
                vs(e, e.reciprocal(sx_tm[:, j:j + 1], st8_f[:, j:j + 1]))
                vs(e, e.reciprocal(isx_tm[:, j:j + 1], sx_tm[:, j:j + 1]))
                e.tensor_scalar(sA7[:], x2nc[:], sx_tm[:, j:j + 1], None, ALU.mult)
                e.tensor_scalar(sB7[:], sA7[:], M1, M1, ALU.add, ALU.subtract)
                e.tensor_scalar(sC7[:], sA7[:], M2, M2, ALU.add, ALU.subtract)
                e.tensor_scalar(sA7[:], sA7[:], M6, M6, ALU.add, ALU.subtract)
                e.tensor_add(sB7[:], sB7[:], sC7[:])
                e.tensor_sub(sB7[:], sA7[:], sB7[:])
                e.tensor_scalar(rxst[:, (j % 2) * D:(j % 2 + 1) * D], sB7[:],
                                -127.0, 128.0, ALU.max, ALU.min).then_inc(v7, 1)

        @block.gpsimd
        def _(e):
            for j in range(8):
                e.wait_ge(v7, j + 1)
                e.dma_start(out=d_rx[j * 128:(j + 1) * 128, :],
                            in_=rxst[:, (j % 2) * D:(j % 2 + 1) * D]).then_inc(g7, 16)
            for j in range(8):
                e.dma_start(out=d_x2[j * 128:(j + 1) * 128, :],
                            in_=x2sb[:, j * D:(j + 1) * D]).then_inc(g7, 16)
            if _DBG:
                for j in range(8):
                    e.dma_start(out=d_dbg_x2[j * 128:(j + 1) * 128, :],
                                in_=x2sb[:, j * D:(j + 1) * D]).then_inc(g7, 16)
                e.wait_ge(g7, 384)
            else:
                e.wait_ge(g7, 256)

        barrier()
        emit_flatten(isx_tm[:], isx_row[:], 0)
        emit_bcast(isx_row[:], cb0)
        barrier()
        A.free("dscr7", "x2nc", "sA7", "sB7", "sC7", "rxst")
        A.free("x2sb")

        rx_fm = A.alloc("rx_fm", [128, 16 * TQ], BF16)

        wst = A.alloc("wst", [128, 2 * D], F32)
        absf = A.alloc("absf", [128, 2 * D], F32)
        absb = A.alloc("absb", [128, 2 * D], BF16)
        absl = A.alloc("absl", [128, 2 * D], BF16)
        abw = A.alloc("abw", [128, D], F32)
        ab9 = A.alloc("ab9", [1, 512], F32)
        q8 = sem()
        q9 = sem()
        t9 = sem()
        sA8 = sem()
        v9 = sem()
        pP = sem()
        s9w = sem()
        v9b = sem()

        @block.sync
        def _(e):
            for hc in range(16):
                e.dma_start_transpose(rx_fm[:, hc * TQ:(hc + 1) * TQ],
                                      d_rx[:, hc * 128:(hc + 1) * 128]).then_inc(q8, 16)
            for i in range(192):
                if i >= 2:
                    e.wait_ge(v9, i - 1)
                if i < 64:
                    src = d_wgp[i * D:(i + 1) * D, :]
                elif i < 128:
                    src = d_wup[(i - 64) * D:(i - 63) * D, :]
                else:
                    src = d_wdp[(i - 128) * D:(i - 127) * D, :]
                wload(e, wst[:, (i % 2) * D:(i % 2 + 1) * D], src, 128, 128).then_inc(q9, 16)

        @block.scalar
        def _(e):
            for i in range(192):
                e.wait_ge(q9, 16 * (i + 1))
                if i >= 2:
                    e.wait_ge(pP, i - 1)
                e.activation(absf[:, (i % 2) * D:(i % 2 + 1) * D],
                             wst[:, (i % 2) * D:(i % 2 + 1) * D],
                             AF.Abs, bias=c_zero[:]).then_inc(sA8, 1)
                if i % 64 == 63:
                    e.wait_ge(t9, i + 1)
                    if i >= 64:
                        e.wait_ge(v9b, i // 64)
                    e.copy(ab9[:], psR[0:1, 0:512]).then_inc(s9w, 1)
            e.wait_ge(t9, 193)
            e.copy(bc16[:], psX[:, 0:16])

        @block.gpsimd
        def _(e):
            for i in range(192):
                e.wait_ge(sA8, i + 1)
                if i >= 2:
                    e.wait_ge(t9, i - 1)
                sl8 = slice((i % 2) * D, (i % 2 + 1) * D)
                e.tensor_copy(absb[:, sl8], absf[:, sl8])
                e.tensor_copy(abw[:], absb[:, sl8])
                e.tensor_tensor(absl[:, sl8], absf[:, sl8], abw[:],
                                ALU.subtract).then_inc(v9, 1)
                e.engine_nop().then_inc(pP, 1)

        @block.vector
        def _(e):
            for w in range(3):
                e.wait_ge(s9w, w + 1)
                for lv in (256, 128, 64, 32, 16, 8, 4, 2, 1):
                    vs(e, e.tensor_add(ab9[0:1, 0:lv], ab9[0:1, 0:lv],
                                       ab9[0:1, lv:2 * lv]))
                vs(e, e.tensor_copy(m_sc[0:1, w:w + 1], ab9[0:1, 0:1]))
                e.engine_nop().then_inc(v9b, 1)
            vs(e, e.tensor_scalar_mul(m_sc[0:1, 3:6], m_sc[0:1, 0:3],
                                      1.0 / (float(D) * float(FI))))
            vs(e, e.tensor_scalar_max(m_sc[0:1, 3:6], m_sc[0:1, 3:6], 1e-5))
            vs(e, e.tensor_scalar_mul(m_sc[0:1, 6:9], m_sc[0:1, 3:6], 0.5))
            vs(e, e.tensor_scalar_mul(m_sc[0:1, 9:12], m_sc[0:1, 3:6], -0.5))
            vs(e, e.tensor_scalar_mul(m_sc[0:1, 12:15], m_sc[0:1, 3:6], -1.0))
            vs(e, e.tensor_copy(r16h[:], m_sc[:]))
            vs(e, e.tensor_copy(r16f[:], r16h[:]))
            vs(e, e.tensor_sub(r16g[:], m_sc[:], r16f[:]))
            vs(e, e.tensor_copy(r16m[:], r16g[:]))
            vs(e, e.tensor_copy(r16f[:], r16m[:]))
            vs(e, e.tensor_sub(r16g[:], r16g[:], r16f[:]))
            vs(e, e.tensor_copy(r16l[:], r16g[:]))
            e.engine_nop().then_inc(v9b, 10)

        @block.tensor
        def _(e):
            for i in range(192):
                e.wait_ge(v9, i + 1)
                if i % 64 == 0 and i > 0:
                    e.wait_ge(v9b, i // 64)
                for tt in range(4):
                    e.matmul(psR[0:1, 0:512], onesK[:],
                             absb[:, (i % 2) * D + tt * 512:(i % 2) * D + (tt + 1) * 512],
                             start=(i % 64 == 0 and tt == 0), stop=False)
                    m = e.matmul(psR[0:1, 0:512], onesK[:],
                                 absl[:, (i % 2) * D + tt * 512:(i % 2) * D + (tt + 1) * 512],
                                 start=False, stop=(i % 64 == 63 and tt == 3))
                m.then_inc(t9, 1)
            e.wait_ge(v9b, 13)
            e.matmul(psX[:, 0:16], ones1[0:1, :], r16h[0:1, :], start=True, stop=False)
            e.matmul(psX[:, 0:16], ones1[0:1, :], r16m[0:1, :], start=False, stop=False)
            e.matmul(psX[:, 0:16], ones1[0:1, :], r16l[0:1, :],
                     start=False, stop=True).then_inc(t9, 1)

        if _DBG:
            gd8 = sem()

            @block.gpsimd
            def _(e):
                e.wait_ge(t9, 193)
                e.dma_start(out=d_dbg_p8[0:1, 0:16], in_=m_sc[:]).then_inc(gd8, 16)
                e.dma_start(out=d_dbg_p8[1:2, 0:512], in_=ab9[:]).then_inc(gd8, 16)
                e.dma_start(out=d_dbg_p8[2:3, 0:D],
                            in_=absb[0:1, 0:D]).then_inc(gd8, 16)
                e.dma_start(out=d_dbg_p8[3:4, 0:128].rearrange("o (p q) -> (o p) q", q=1),
                            in_=onesK[:]).then_inc(gd8, 16)
                e.wait_ge(gd8, 64)

        barrier()
        A.free("wst", "absf", "absb", "absl", "abw", "ab9")

        pga = A.alloc("pga", [128, 2 * D], F32)
        pgb = A.alloc("pgb", [128, 2 * D], F32)
        rwg = A.alloc("rwg", [128, 2 * D], BF16)
        rwu = A.alloc("rwu", [128, 2 * D], BF16)
        qs1 = A.alloc("qs1", [128, D], BF16)
        qs2 = A.alloc("qs2", [128, D], BF16)
        gsc = A.alloc("gsc", [128, TQ], F32)
        usc = A.alloc("usc", [128, TQ], F32)
        sgl = A.alloc("sgl", [128, TQ], F32)
        hsb = A.alloc("hsb", [128, 2 * TQ], F32)
        hsq = A.alloc("hsq", [128, 2 * TQ], BF16)
        zb = A.alloc("zb", [128, TQ], F32)
        hf2 = A.alloc("hf2", [128, 2 * TQ], F32)
        hfq = A.alloc("hfq", [128, TQ], F32)
        hsqlo = A.alloc("hsqlo", [128, 2 * TQ], BF16)
        qW = sem()
        vq = sem()
        tM = sem()
        sG = sem()
        sU = sem()
        vG = sem()
        vE = sem()
        sS = sem()
        vH = sem()
        pH = sem()
        gH = sem()
        tR = sem()

        @block.sync
        def _(e):
            for oc in range(64):
                if oc >= 2:
                    e.wait_ge(vq, 2 * (oc - 1))
                wload(e, pga[:, (oc % 2) * D:(oc % 2 + 1) * D],
                      d_wgp[oc * D:(oc + 1) * D, :], 128, 128).then_inc(qW, 16)
                wload(e, pgb[:, (oc % 2) * D:(oc % 2 + 1) * D],
                      d_wup[oc * D:(oc + 1) * D, :], 128, 128).then_inc(qW, 16)

        @block.vector
        def _(e):
            for oc in range(64):
                e.wait_ge(qW, 32 * (oc + 1))
                if oc >= 2:
                    e.wait_ge(tM, 4 * (oc - 1))
                sl = slice((oc % 2) * D, (oc % 2 + 1) * D)
                e.tensor_scalar(qs1[:], pga[:, sl], bc16[:, 6:7], -0.5, ALU.is_ge, ALU.add)
                e.tensor_scalar(qs2[:], pga[:, sl], bc16[:, 9:10], -0.5, ALU.is_gt, ALU.add)
                e.tensor_add(rwg[:, sl], qs1[:], qs2[:])
                e.tensor_scalar(qs1[:], pgb[:, sl], bc16[:, 7:8], -0.5, ALU.is_ge, ALU.add)
                e.tensor_scalar(qs2[:], pgb[:, sl], bc16[:, 10:11], -0.5, ALU.is_gt, ALU.add)
                e.tensor_add(rwu[:, sl], qs1[:], qs2[:]).then_inc(vq, 2)
                e.wait_ge(sG, 2 * (oc + 1))
                e.tensor_mul(gsc[:], gsc[:], cb0[:]).then_inc(vG, 1)
                e.wait_ge(sU, 2 * (oc + 1))
                e.tensor_mul(usc[:], usc[:], cb0[:])
                if oc >= 2:
                    e.wait_ge(pH, oc - 1)
                    e.wait_ge(gH, (32 if _DBG else 16) * (oc - 1))
                e.wait_ge(sS, oc + 1)
                e.tensor_mul(hsb[:, (oc % 2) * TQ:(oc % 2 + 1) * TQ],
                             sgl[:], usc[:]).then_inc(vH, 1)
                e.scalar_tensor_tensor(zb[:], hsb[:, (oc % 2) * TQ:(oc % 2 + 1) * TQ],
                                       lnw2[:, oc:oc + 1],
                                       hsb[:, (oc % 2) * TQ:(oc % 2 + 1) * TQ],
                                       ALU.mult, ALU.mult)
                e.tensor_max(zm_acc[:], zm_acc[:], zb[:])

        @block.scalar
        def _(e):
            for oc in range(64):
                e.wait_ge(tM, 4 * oc + 2)
                if oc >= 1:
                    e.wait_ge(vH, oc)
                for tt in range(2):
                    e.activation(gsc[:, tt * 512:(tt + 1) * 512],
                                 psG[:, tt * 512:(tt + 1) * 512], AF.Identity,
                                 bias=c_zero[:], scale=bc16[:, 12:13]).then_inc(sG, 1)
                e.wait_ge(tM, 4 * oc + 4)
                for tt in range(2):
                    e.activation(usc[:, tt * 512:(tt + 1) * 512],
                                 psU[:, tt * 512:(tt + 1) * 512], AF.Identity,
                                 bias=c_zero[:], scale=bc16[:, 13:14]).then_inc(sU, 1)
                e.wait_ge(vG, oc + 1)
                e.activation(sgl[:], gsc[:], AF.Silu).then_inc(sS, 1)

        @block.gpsimd
        def _(e):
            for oc in range(64):
                e.wait_ge(vH, oc + 1)
                e.dma_start(out=d_h[oc * 128:(oc + 1) * 128, :],
                            in_=hsb[:, (oc % 2) * TQ:(oc % 2 + 1) * TQ]).then_inc(gH, 16)
                if _DBG:
                    e.dma_start(out=d_dbg_h[oc * 128:(oc + 1) * 128, :],
                                in_=hsb[:, (oc % 2) * TQ:(oc % 2 + 1) * TQ]).then_inc(gH, 16)
                if oc >= 2:
                    e.wait_ge(tR, oc - 1)
                sl = slice((oc % 2) * TQ, (oc % 2 + 1) * TQ)
                e.tensor_tensor(hf2[:, sl], hsb[:, sl], hsb[:, sl], ALU.mult)
                e.tensor_copy(hsq[:, sl], hf2[:, sl])
                e.tensor_copy(hfq[:], hsq[:, sl])
                e.tensor_tensor(hsqlo[:, sl], hf2[:, sl], hfq[:],
                                ALU.subtract).then_inc(pH, 1)
            e.wait_ge(gH, 64 * 32 if _DBG else 64 * 16)

        @block.tensor
        def _(e):
            e.wait_ge(q8, 256)
            for oc in range(64):
                e.wait_ge(vq, 2 * (oc + 1))
                if oc >= 1:
                    e.wait_ge(sU, 2 * oc)
                for hc in range(16):
                    for tt in range(2):
                        m = e.matmul(psG[:, tt * 512:(tt + 1) * 512],
                                     rwg[:, (oc % 2) * D + hc * 128:(oc % 2) * D + (hc + 1) * 128],
                                     rx_fm[:, hc * TQ + tt * 512:hc * TQ + (tt + 1) * 512],
                                     start=(hc == 0), stop=(hc == 15))
                        if hc == 15:
                            m.then_inc(tM, 1)
                for hc in range(16):
                    for tt in range(2):
                        m = e.matmul(psU[:, tt * 512:(tt + 1) * 512],
                                     rwu[:, (oc % 2) * D + hc * 128:(oc % 2) * D + (hc + 1) * 128],
                                     rx_fm[:, hc * TQ + tt * 512:hc * TQ + (tt + 1) * 512],
                                     start=(hc == 0), stop=(hc == 15))
                        if hc == 15:
                            m.then_inc(tM, 1)
                if oc >= 1:
                    e.wait_ge(pH, oc)
                    po = (oc - 1) % 2
                    for tt in range(2):
                        e.matmul(psR[0:1, tt * 512:(tt + 1) * 512], onesK[:],
                                 hsq[:, po * TQ + tt * 512:po * TQ + (tt + 1) * 512],
                                 start=(oc - 1 == 0), stop=False)
                        m = e.matmul(psR[0:1, tt * 512:(tt + 1) * 512], onesK[:],
                                     hsqlo[:, po * TQ + tt * 512:po * TQ + (tt + 1) * 512],
                                     start=False, stop=False)
                    m.then_inc(tR, 1)
            e.wait_ge(pH, 64)
            for tt in range(2):
                e.matmul(psR[0:1, tt * 512:(tt + 1) * 512], onesK[:],
                         hsq[:, (63 % 2) * TQ + tt * 512:(63 % 2) * TQ + (tt + 1) * 512],
                         start=False, stop=False)
                m = e.matmul(psR[0:1, tt * 512:(tt + 1) * 512], onesK[:],
                             hsqlo[:, (63 % 2) * TQ + tt * 512:(63 % 2) * TQ + (tt + 1) * 512],
                             start=False, stop=True)
            m.then_inc(tR, 1)

        barrier()
        A.free("pga", "pgb", "rwg", "rwu", "qs1", "qs2", "gsc", "usc", "sgl", "hsb", "hsq", "zb",
               "hf2", "hfq", "hsqlo")
        A.free("rx_fm")

        sq11a = sem()
        sq11 = sem()
        v11 = sem()
        v11z = sem()
        s11 = sem()
        zmh = A.alloc("zmh", [128, TQ], BF16)
        zml = A.alloc("zml", [128, TQ], BF16)
        zb2 = A.alloc("zb2", [128, TQ], F32)

        @block.vector
        def _(e):
            e.tensor_copy(zmh[:], zm_acc[:])
            e.tensor_copy(zb2[:], zmh[:])
            e.tensor_sub(zml[:], zm_acc[:], zb2[:]).then_inc(v11z, 1)

        @block.gpsimd
        def _(e):
            e.wait_ge(v11z, 1)
            e.dma_start(out=d_zx[0:128, :], in_=zmh[:]).then_inc(sq11a, 16)
            e.dma_start(out=d_zx[128:256, :], in_=zml[:]).then_inc(sq11a, 16)

        zxt = A.alloc("zxt", [128, 8 * 128], BF16)
        zxt2 = A.alloc("zxt2", [128, 8 * 128], BF16)
        zxf = A.alloc("zxf", [128, 128], F32)
        zxf2 = A.alloc("zxf2", [128, 128], F32)

        @block.sync
        def _(e):
            e.wait_ge(sq11a, 32)
            for j in range(8):
                e.dma_start_transpose(zxt[:, j * 128:(j + 1) * 128],
                                      d_zx[0:128, j * 128:(j + 1) * 128]).then_inc(sq11, 16)
                e.dma_start_transpose(zxt2[:, j * 128:(j + 1) * 128],
                                      d_zx[128:256, j * 128:(j + 1) * 128]).then_inc(sq11, 16)

        @block.scalar
        def _(e):
            for tt in range(2):
                e.activation(row_sc[0:1, tt * 512:(tt + 1) * 512],
                             psR[0:1, tt * 512:(tt + 1) * 512], AF.Identity,
                             bias=c_eps6[0:1, :], scale=c_ifi[0:1, :])
            e.activation(row_f[:], row_sc[:], AF.Ln, bias=c_zero[0:1, :])
            e.activation(row_sc2[:], row_f[:], AF.Exp, bias=c_zero[0:1, :],
                         scale=c_nhalf[0:1, :]).then_inc(s11, 1)

        @block.vector
        def _(e):
            e.wait_ge(s11, 1)
            e.tensor_mul(row_f[:], row_sc2[:], row_sc2[:])
            e.tensor_mul(row_f[:], row_f[:], row_sc[:])
            e.tensor_scalar(row_f[:], row_f[:], -0.5, 1.5, ALU.mult, ALU.add)
            e.tensor_mul(rstd_row[:], row_sc2[:], row_f[:])
            e.wait_ge(sq11, 256)
            for j in range(8):
                e.tensor_copy(zxf[:], zxt[:, j * 128:(j + 1) * 128])
                vs(e, e.tensor_copy(zxf2[:], zxt2[:, j * 128:(j + 1) * 128]))
                vs(e, e.tensor_add(zxf[:], zxf[:], zxf2[:]))
                vs(e, e.reduce_max(zm_tm[:, j:j + 1], zxf[:], AX.X))
            e.engine_nop().then_inc(v11, 1)

        barrier()
        emit_flatten(zm_tm[:], zx_row[:], 24)
        barrier()
        v11b = sem()
        s11b = sem()

        @block.scalar
        def _(e):
            e.activation(row_f[:], zx_row[:], AF.Ln, bias=c_tiny[0:1, :])
            e.activation(row_sc[:], row_f[:], AF.Exp, bias=c_zero[0:1, :],
                         scale=c_half[0:1, :]).then_inc(s11b, 1)

        @block.vector
        def _(e):
            e.wait_ge(s11b, 1)
            vs(e, e.tensor_mul(row_sc[:], rstd_row[:], row_sc[:]))
            vs(e, e.tensor_scalar_max(row_sc[:], row_sc[:], 1e-5))
            vs(e, e.tensor_scalar_mul(row_sc[:], row_sc[:], 1.0 / 127.0))
            vs(e, e.reciprocal(row_sc2[:], row_sc[:]))
            vs(e, e.reciprocal(invsh_row[:], row_sc2[:]))
            vs(e, e.tensor_mul(cb1_row[:], rstd_row[:], row_sc2[:]))
            vs(e, e.tensor_scalar(cd_row[:], invsh_row[:], m_sc[0:1, 14:15], None, ALU.mult))
            e.engine_nop().then_inc(v11b, 1)

        @block.tensor
        def _(e):
            e.wait_ge(v11b, 1)

        if _DBG:
            gd11 = sem()

            @block.gpsimd
            def _(e):
                e.wait_ge(v11b, 1)
                e.dma_start(out=d_dbg_st[0:1, :], in_=cb1_row[:]).then_inc(gd11, 16)
                e.dma_start(out=d_dbg_st[1:2, :], in_=invsh_row[:]).then_inc(gd11, 16)
                e.dma_start(out=d_dbg_st[2:3, 0:16], in_=m_sc[:]).then_inc(gd11, 16)
                e.wait_ge(gd11, 48)

        emit_bcast(cb1_row[:], cb1)
        emit_bcast(cd_row[:], cdt)
        barrier()
        if _DBG:
            gd11b = sem()

            @block.gpsimd
            def _(e):
                e.dma_start(out=d_dbg_cbt[0:128, :], in_=cb0[:]).then_inc(gd11b, 16)
                e.dma_start(out=d_dbg_cbt[128:256, :], in_=cb1[:]).then_inc(gd11b, 16)
                e.dma_start(out=d_dbg_cbt[256:384, :], in_=cdt[:]).then_inc(gd11b, 16)
                e.wait_ge(gd11b, 48)

            barrier()
        A.free("zxt", "zxt2", "zxf", "zxf2", "zmh", "zml", "zb2")
        A.free("rstd_row", "invsh_row", "isx_row", "cb1_row", "cd_row", "zx_row")

        rh_fm = A.alloc("rh_fm", [128, 64 * TQ], BF16)
        hld = A.alloc("hld", [128, 2 * TQ], F32)
        sA12 = A.alloc("sA12", [128, TQ], F32)
        sB12 = A.alloc("sB12", [128, TQ], F32)
        sC12 = A.alloc("sC12", [128, TQ], F32)
        q12 = sem()
        v12 = sem()

        @block.sync
        def _(e):
            for hc2 in range(64):
                if hc2 >= 2:
                    e.wait_ge(v12, hc2 - 1)
                e.dma_start(out=hld[:, (hc2 % 2) * TQ:(hc2 % 2 + 1) * TQ],
                            in_=d_h[hc2 * 128:(hc2 + 1) * 128, :]).then_inc(q12, 16)

        @block.vector
        def _(e):
            for hc2 in range(64):
                e.wait_ge(q12, 16 * (hc2 + 1))
                e.scalar_tensor_tensor(sA12[:], hld[:, (hc2 % 2) * TQ:(hc2 % 2 + 1) * TQ],
                                       lnw[:, hc2:hc2 + 1], cb1[:], ALU.mult, ALU.mult)
                e.tensor_scalar(sB12[:], sA12[:], M1, M1, ALU.add, ALU.subtract)
                e.tensor_scalar(sC12[:], sA12[:], M2, M2, ALU.add, ALU.subtract)
                e.tensor_scalar(sA12[:], sA12[:], M6, M6, ALU.add, ALU.subtract)
                e.tensor_add(sB12[:], sB12[:], sC12[:])
                e.tensor_sub(sB12[:], sA12[:], sB12[:])
                e.tensor_scalar(rh_fm[:, hc2 * TQ:(hc2 + 1) * TQ], sB12[:],
                                -127.0, 128.0, ALU.max, ALU.min).then_inc(v12, 1)

        if _DBG:
            gd12 = sem()

            @block.gpsimd
            def _(e):
                for hc2 in range(64):
                    e.wait_ge(v12, hc2 + 1)
                    e.dma_start(out=d_dbg_rh[hc2 * 128:(hc2 + 1) * 128, :],
                                in_=rh_fm[:, hc2 * TQ:(hc2 + 1) * TQ]).then_inc(gd12, 16)
                e.wait_ge(gd12, 64 * 16)

        barrier()
        A.free("hld", "sA12", "sB12", "sC12")

        pdn = A.alloc("pdn", [128, 2 * D], F32)
        rwd = A.alloc("rwd", [128, 2 * D], BF16)
        qd1 = A.alloc("qd1", [128, D], BF16)
        qd2 = A.alloc("qd2", [128, D], BF16)
        ysb = A.alloc("ysb", [128, 2 * TQ], BF16)
        y13s = A.alloc("y13s", [128, TQ], F32)
        q13 = sem()
        v13 = sem()
        t13 = sem()
        t13q = sem()
        s13 = sem()
        vE13 = sem()
        gY = sem()

        @block.sync
        def _(e):
            for idx in range(64):
                if idx >= 2:
                    e.wait_ge(v13, idx - 1)
                wload(e, pdn[:, (idx % 2) * D:(idx % 2 + 1) * D],
                      d_wdp[idx * D:(idx + 1) * D, :], 128, 128).then_inc(q13, 16)

        @block.vector
        def _(e):
            for idx in range(64):
                e.wait_ge(q13, 16 * (idx + 1))
                if idx >= 2:
                    e.wait_ge(t13q, idx - 1)
                sl = slice((idx % 2) * D, (idx % 2 + 1) * D)
                e.tensor_scalar(qd1[:], pdn[:, sl], bc16[:, 8:9], -0.5, ALU.is_ge, ALU.add)
                e.tensor_scalar(qd2[:], pdn[:, sl], bc16[:, 11:12], -0.5, ALU.is_gt, ALU.add)
                e.tensor_add(rwd[:, sl], qd1[:], qd2[:])
                e.engine_nop().then_inc(v13, 1)
                if idx % 4 == 3:
                    o2c = idx // 4
                    pD = psG if o2c % 2 == 0 else psU
                    e.wait_ge(t13, o2c + 1)
                    e.wait_ge(t13q, 4 * (o2c + 1))
                    if o2c >= 2:
                        e.wait_ge(gY, (32 if _DBG else 16) * (o2c - 1))
                    for tt in range(2):
                        e.tensor_mul(ysb[:, (o2c % 2) * TQ + tt * 512:(o2c % 2) * TQ + (tt + 1) * 512],
                                     pD[:, tt * 512:(tt + 1) * 512],
                                     cdt[:, tt * 512:(tt + 1) * 512])
                    e.engine_nop().then_inc(vE13, 1)

        @block.tensor
        def _(e):
            for o2c in range(16):
                pD = psG if o2c % 2 == 0 else psU
                if o2c >= 2:
                    e.wait_ge(vE13, o2c - 1)
                for q4 in range(4):
                    idx = o2c * 4 + q4
                    e.wait_ge(v13, idx + 1)
                    for c in range(16):
                        h2 = q4 * 16 + c
                        for tt in range(2):
                            m = e.matmul(pD[:, tt * 512:(tt + 1) * 512],
                                         rwd[:, (idx % 2) * D + c * 128:(idx % 2) * D + (c + 1) * 128],
                                         rh_fm[:, h2 * TQ + tt * 512:h2 * TQ + (tt + 1) * 512],
                                         start=(h2 == 0), stop=(h2 == 63))
                            if h2 == 63 and tt == 0:
                                m.then_inc(t13, 1)
                        if c == 15:
                            m.then_inc(t13q, 1)

        @block.gpsimd
        def _(e):
            for o2c in range(16):
                e.wait_ge(vE13, o2c + 1)
                e.dma_start(out=d_y[o2c * 128:(o2c + 1) * 128, :],
                            in_=ysb[:, (o2c % 2) * TQ:(o2c % 2 + 1) * TQ]).then_inc(gY, 16)
                if _DBG:
                    e.dma_start(out=d_dbg_y[o2c * 128:(o2c + 1) * 128, :],
                                in_=ysb[:, (o2c % 2) * TQ:(o2c % 2 + 1) * TQ]).then_inc(gY, 16)
            e.wait_ge(gY, 512 if _DBG else 256)

        barrier()
        A.free("pdn", "rwd", "qd1", "qd2", "ysb", "y13s")
        A.free("rh_fm")

        ytm = A.alloc("ytm", [128, 2 * D], BF16)
        x2st = A.alloc("x2st", [128, 2 * D], F32)
        yf = A.alloc("yf", [128, D], F32)
        outst = A.alloc("outst", [128, 2 * D], F32)
        q14 = sem()
        v14 = sem()
        st14 = sem()

        @block.sync
        def _(e):
            for tc in range(8):
                if tc >= 2:
                    e.wait_ge(v14, tc - 1)
                    e.dma_start(out=d_out[(tc - 2) * 128:(tc - 1) * 128, :],
                                in_=outst[:, (tc % 2) * D:(tc % 2 + 1) * D]).then_inc(st14, 16)
                e.dma_start_transpose(ytm[:, (tc % 2) * D:(tc % 2 + 1) * D],
                                      d_y[:, tc * 128:(tc + 1) * 128]).then_inc(q14, 16)
                e.dma_start(out=x2st[:, (tc % 2) * D:(tc % 2 + 1) * D],
                            in_=d_x2[tc * 128:(tc + 1) * 128, :]).then_inc(q14, 16)
            for tc in (6, 7):
                e.wait_ge(v14, tc + 1)
                e.dma_start(out=d_out[tc * 128:(tc + 1) * 128, :],
                            in_=outst[:, (tc % 2) * D:(tc % 2 + 1) * D]).then_inc(st14, 16)
            e.wait_ge(st14, 128)

        @block.vector
        def _(e):
            for tc in range(8):
                e.wait_ge(q14, 32 * (tc + 1))
                if tc >= 2:
                    e.wait_ge(st14, 16 * (tc - 1))
                sl = slice((tc % 2) * D, (tc % 2 + 1) * D)
                e.tensor_copy(yf[:], ytm[:, sl])
                e.tensor_add(outst[:, sl], x2st[:, sl], yf[:]).then_inc(v14, 1)

    return nc



_theta = np.exp(np.arange(0, RD, 2, dtype=np.float64) * (-np.log(10000.0) / RD))


def _rope_tables(hf):
    pos_q = (np.arange(TQ, dtype=np.float64) + hf * TQ)
    pos_k = (np.arange(L, dtype=np.float64) + hf * TQ) % L
    ang_q = pos_q[None, :] * _theta[:, None]
    ang_k = pos_k[None, :] * _theta[:, None]
    tka = np.ascontiguousarray(np.repeat(np.sin(ang_k), 2, axis=0).astype(np.float32))
    tkb = np.ascontiguousarray(np.repeat(np.cos(ang_k), 2, axis=0).astype(np.float32))
    tqa32 = np.repeat(np.sin(ang_q), 2, axis=0)
    tqb32 = np.repeat(np.cos(ang_q), 2, axis=0)
    tqa = np.ascontiguousarray(np.tile(tqa32, (16, 1)).astype(np.float32))
    tqb = np.ascontiguousarray(np.tile(tqb32, (16, 1)).astype(np.float32))
    return tqa, tqb, tka, tkb


def _rot_mats():
    m = np.zeros((RD, RD), np.float32)
    for i in range(RD // 2):
        m[2 * i + 1, 2 * i] = -1.0
        m[2 * i, 2 * i + 1] = 1.0
    pk = np.ascontiguousarray(m.astype(BF))

    def pq(heads):
        a = np.zeros((128, 128), np.float32)
        r = np.zeros((128, 128), np.float32)
        for slot, a2 in enumerate(heads):
            for dd in range(RD):
                a[a2 * RD + dd, slot * 64 + dd] = 1.0
            r[a2 * RD:(a2 + 1) * RD, slot * 64:slot * 64 + RD] = m
        return (np.ascontiguousarray(a.astype(BF)), np.ascontiguousarray(r.astype(BF)))

    pqe_a, pqe_r = pq((0, 1))
    pqo_a, pqo_r = pq((2, 3))
    return pk, pqe_a, pqe_r, pqo_a, pqo_r


def kernel(L0_attn_h, L0_attn_W_dkv, L0_attn_b_dkv, L0_attn_W_uk, L0_attn_b_uk,
           L0_attn_W_uv, L0_attn_b_uv, L0_attn_W_dq, L0_attn_b_dq, L0_attn_W_uq,
           L0_attn_b_uq, L0_attn_W_qr, L0_attn_b_qr, L0_attn_W_kr, L0_attn_b_kr,
           L0_attn_W_fc, L0_attn_b_fc, L0_ffn_w_gate, L0_ffn_w_up, L0_ffn_w_down,
           L0_ffn_ln_weight):
    f32 = lambda a: np.ascontiguousarray(np.asarray(a, np.float32))
    x = f32(L0_attn_h)
    wg = f32(L0_ffn_w_gate)
    wu = f32(L0_ffn_w_up)
    wd = f32(L0_ffn_w_down)

    shared = {
        "wdkv": f32(L0_attn_W_dkv), "wuk": f32(L0_attn_W_uk), "wuv": f32(L0_attn_W_uv),
        "wdq": f32(L0_attn_W_dq), "wuq": f32(L0_attn_W_uq), "wqr": f32(L0_attn_W_qr),
        "wkr": f32(L0_attn_W_kr), "wfc": f32(L0_attn_W_fc),
        "bdkv_t": f32(np.asarray(L0_attn_b_dkv).reshape(4, 128).T),
        "buk_t": f32(np.asarray(L0_attn_b_uk).reshape(8, 128).T),
        "bdq_t": f32(np.asarray(L0_attn_b_dq).reshape(4, 128).T),
        "buq_t": f32(np.asarray(L0_attn_b_uq).reshape(8, 128).T),
        "bqr_t": f32(np.asarray(L0_attn_b_qr).reshape(4, 128).T),
        "bkr_t": f32(np.asarray(L0_attn_b_kr).reshape(RD, 1)),
        "buv_r": f32(np.asarray(L0_attn_b_uv).reshape(1, UPD)),
        "bfc_r": f32(np.asarray(L0_attn_b_fc).reshape(1, D)),
        "wg_p": f32(wg.T.reshape(D, 64, 128).transpose(1, 0, 2).reshape(64 * D, 128)),
        "wu_p": f32(wu.T.reshape(D, 64, 128).transpose(1, 0, 2).reshape(64 * D, 128)),
        "wd_p": f32(wd.T.reshape(FI, 16, 128).transpose(1, 0, 2).reshape(16 * FI, 128)),
        "lnw_t": f32(np.asarray(L0_ffn_ln_weight).reshape(64, 128).T),
        "identb_in": np.ascontiguousarray(np.eye(128, dtype=np.float32).astype(BF)),
    }
    pk, pqe_a, pqe_r, pqo_a, pqo_r = _rot_mats()
    shared.update({"pk_rt": pk, "pqe_id": pqe_a, "pqe_rt": pqe_r,
                   "pqo_id": pqo_a, "pqo_rt": pqo_r})
    tabs = {hf: _rope_tables(hf) for hf in (0, 1)}

    nc = build_nc()
    in_maps = []
    for c in range(8):
        b, hf = c // 2, c % 2
        tqa, tqb, tka, tkb = tabs[hf]
        m = dict(shared)
        m["x"] = np.ascontiguousarray(np.roll(x[b], -hf * TQ, axis=0))
        m.update({"tqa": tqa, "tqb": tqb, "tka": tka, "tkb": tkb})
        in_maps.append(m)

    res = run_bass_kernel_spmd(nc, in_maps, core_ids=list(range(8)))
    out = np.empty((B, L, D), np.float32)
    for c in range(8):
        b, hf = c // 2, c % 2
        out[b, hf * TQ:(hf + 1) * TQ, :] = np.asarray(res.results[c]["out"], np.float32)
    return out
